# version 02
# baseline (speedup 1.0000x reference)
import contextlib
import os

import numpy as np
import ml_dtypes

import concourse.bass as bass
import concourse.mybir as mybir
from concourse.alu_op_type import AluOpType as A
from concourse.bass_utils import run_bass_kernel_spmd

F32 = mybir.dt.float32
F32R = mybir.dt.float32r
BF16 = mybir.dt.bfloat16
I32 = mybir.dt.int32
AF = mybir.ActivationFunctionType
AX = mybir.AxisListType.X
BF = ml_dtypes.bfloat16
ET = mybir.EngineType

D = 2048
TOK = 2048
MT = 512
NMT = 4
I8K = 8192
MAGIC = 1163030402
QSRC_ROWS = 16384
QTILES = 128
EPS_BL = 1e-8
EPS_GN = 1e-5
EPS_LN = 1e-5
CLIP_MIN = 1e-5
ISQ_D = float(1.0 / np.sqrt(np.float64(D)))
ISQ_I = float(1.0 / np.sqrt(np.float64(I8K)))

W_ORDER = ["wi", "wf", "wg", "wo", "wgate", "wup", "wdown"]
W_NUMEL = {"wi": D * D, "wf": D * D, "wg": D * D, "wo": D * D,
           "wgate": D * I8K, "wup": D * I8K, "wdown": I8K * D}
QROWS = {"wi": 1024, "wf": 1024, "wg": 1024, "wo": 1024,
         "wgate": 4096, "wup": 4096, "wdown": 4096}
TILE_W = [(w, j * 128) for w in W_ORDER for j in range(QROWS[w] // 128)]

STAGE = int(os.environ.get("KSTAGE", "4"))
DEBUG_DUMPS = os.environ.get("KDEBUG", "0") == "1"


def _hadamard(n):
    H = np.array([[1.0]], np.float32)
    while H.shape[0] < n:
        H = np.block([[H, H], [H, -H]]).astype(np.float32)
    return H


class VInstr:
    def __init__(self, b, eng, fn, args, kwargs):
        self.b, self.incs = b, []
        self.fn, self.args, self.kwargs = fn, args, kwargs
        b.plan[eng].append(self)

    def inc(self, name, k):
        h = self.b.sem(name)
        tot = self.b.bump(name, k)
        self.incs.append((h, k, tot))
        return self

    def replay(self, e):
        instr = getattr(e, self.fn)(*self.args, **self.kwargs)
        for h, k, _ in self.incs:
            instr = instr.then_inc(h, k)


class Raw:
    def __init__(self, b, eng, f):
        self.f = f
        b.plan[eng].append(self)

    def replay(self, e):
        self.f(e)


_TE_MERGE = {"swi_mm", "swf_mm", "swg_mm", "wo_mm", "fg_mm", "fu_mm", "dw_mm",
             "osmm", "modmm", "ff_mm", "fh_mm", "scT_mm", "wsum_mm",
             "wmean_mm", "gsmm", "tpm6", "tpm7", "cr_t", "rc_t", "nf_mm",
             "rpm0", "rpm1", "rpm2", "rpm3", "bcbm0", "bcbm1", "of_t",
             "ff_t", "fh_t"}
_SC_MERGE = {"ln_sq", "ln_sqrt", "ln_t", "rms_sq", "rms_sqrt", "q_cvt",
             "silu_c", "ep_si", "ep_fg", "gsq", "o_si", "osq", "gm_sq",
             "or_sq", "ff_abs", "fh_abs", "fe_si", "q3c", "q4c", "q5c",
             "wabs_s", "l2_sq", "l2_sqrt", "l2_t"}
_V_MERGE = {"init_v", "wsum_row", "wmean", "wthr", "scT_ev", "mod_ev",
            "wq_tern", "vst", "vecs_done", "alloc_init", "reps_done",
            "reps3a_done", "h1", "ln_v", "ln_ab", "rms_v", "q_y", "q_done",
            "tpe6", "tpe7", "wabs_done", "wo_ev", "sp_cp", "gspc", "gev",
            "q3y", "q4y", "q5y", "q3d", "q4d", "q5d", "ep_it", "ep_ft",
            "bfly", "bfly2", "ff_fold", "fh_fold", "of_tv", "ff_tv", "fh_tv",
            "omx_c", "fmx_c", "vmx_c", "ari_c", "arf_c", "arg_c", "cr_r",
            "rc_r", "bcbe0", "bcbe1", "rpe0", "rpe1", "rp_free", "nf_free",
            "nf_ev", "own_summ", "aux_summ", "carry_done", "p1fin", "p3afin",
            "p3bfin", "p3a_all", "p3b_all", "gm_v", "gm_row", "or_v",
            "ob_row", "oa_row", "fb_row", "bg_row", "vb_row", "va_row",
            "g_done", "l2_v", "l2_ab", "gnwb_ev", "nofb_ev"}
_MERGE = {}
for _n in _TE_MERGE:
    _MERGE[_n] = "temm"
for _n in _SC_MERGE:
    _MERGE[_n] = "scm"
for _n in _V_MERGE:
    _MERGE[_n] = "vcm"
for _n in ("swi_ev", "swf_ev", "swg_ev"):
    _MERGE[_n] = "sw_ev"
for _n in ("ff_ev", "fh_ev"):
    _MERGE[_n] = "hh_ev"
for _n in ("fg_ev", "fu_ev"):
    _MERGE[_n] = "fue"
for _n in ("o_fx", "o_od", "o_done"):
    _MERGE[_n] = "ost"


class B:
    ENGS = ["sy", "sc", "v", "te", "gp"]

    def __init__(self, es, nc):
        self.es, self.nc = es, nc
        self.plan = {e: [] for e in self.ENGS}
        self.sems = {}

    def _p(self, name):
        return _MERGE.get(name, name)

    def sem(self, name):
        name = self._p(name)
        if name not in self.sems:
            self.sems[name] = [self.es.enter_context(self.nc.semaphore(name)), 0]
        return self.sems[name][0]

    def bump(self, name, k):
        name = self._p(name)
        self.sem(name)
        self.sems[name][1] += k
        return self.sems[name][1]

    def val(self, name):
        name = self._p(name)
        self.sem(name)
        return self.sems[name][1]

    def vbar(self, eng):
        self.wait(eng, "vcm")

    SHORT = 512

    def _autoedge(self, eng):
        prev = getattr(self, "_prev", {}).get(eng)
        if prev is None:
            return
        vi, is_short = prev
        if not is_short:
            return
        if vi is None:
            if eng == "v":
                vi = VInstr(self, eng, "memset", (self.mkv, 0.0), {})
            else:
                vi = VInstr(self, eng, "activation",
                            (self.mks, self.mks, AF.Identity),
                            {"bias": 0.0, "scale": 0.0})
        if not vi.incs:
            vi.inc("ae_" + eng, 1)
        h, k, tot = vi.incs[0]
        Raw(self, eng, lambda e, h=h, t=tot: e.wait_ge(h, t))
        self._prev[eng] = (vi, False)

    def _note(self, eng, vi, args):
        if not hasattr(self, "_prev"):
            self._prev = {}
        n = 0
        if args:
            sh = getattr(args[0], "shape", None)
            if sh is not None and len(sh) >= 2:
                n = 1
                for d in sh[1:]:
                    n *= d
        self._prev[eng] = (vi, n < self.SHORT)

    def op(self, eng, fn, *args, **kwargs):
        if fn == "nop" and eng == "v":
            fn, args, kwargs = "memset", (self.mkv, 0.0), {}
        elif fn == "nop" and eng == "sc":
            fn, args, kwargs = "activation", (self.mks, self.mks, AF.Identity), \
                {"bias": 0.0, "scale": 0.0}
        if eng in ("v", "sc"):
            self._autoedge(eng)
        vi = VInstr(self, eng, fn, args, kwargs)
        if eng in ("v", "sc"):
            self._note(eng, vi, args)
        return vi

    def tinc(self, name, k=1):
        h = self.sem(name)
        self.bump(name, k)
        return h, k

    def wait(self, eng, name, v=None):
        h = self.sem(name)
        vv = self.val(name) if v is None else v
        Raw(self, eng, lambda e: e.wait_ge(h, vv))

    def dma(self, eng, out, in_, semname):
        return self.op(eng, "dma_start", out=out, in_=in_).inc(semname, 16)

    def raw(self, eng, f):
        if eng in ("v", "sc"):
            self._autoedge(eng)
            if not hasattr(self, "_prev"):
                self._prev = {}
            self._prev[eng] = (None, True)
        Raw(self, eng, f)

    def sb(self, name, shape, dtype):
        return self.es.enter_context(self.nc.sbuf_tensor(name, shape, dtype))

    def finalize(self, blk):
        for ename, attr in (("sy", "sync"), ("sc", "scalar"), ("v", "vector"),
                            ("te", "tensor"), ("gp", "gpsimd")):
            items = self.plan[ename]

            def runner(e, items=items):
                for it in items:
                    it.replay(e)
            getattr(blk, attr)(runner)


def build_program(debug_names=()):
    es = contextlib.ExitStack()
    with es:
        nc = bass.Bass("TRN2", target_bir_lowering=False, debug=False,
                       detect_race_conditions=False)
        b = B(es, nc)
        sbt, op, wait, dma, raw = b.sb, b.op, b.wait, b.dma, b.raw

        EI = lambda n, s, t=F32: nc.dram_tensor(n, s, t, kind="ExternalInput").ap()
        xo = EI("xo", [TOK, D])
        xa = EI("xa", [TOK, D])
        qsrc = EI("qsrc", [QSRC_ROWS, D])
        wmod_sl = EI("wmod_sl", [D, 6144])
        cvec = EI("cvec", [2, D])
        norms5 = EI("norms5", [5, D])
        constf = EI("constf", [128, 256])
        maskv = EI("maskv", [128, 8])
        rolei = EI("rolei", [1, 8], I32)
        y_out = nc.dram_tensor("y", [TOK, D], F32, kind="ExternalOutput").ap()

        dbg = {}
        for nm, shape, dt_ in debug_names:
            dbg[nm] = nc.dram_tensor(nm, shape, dt_, kind="ExternalOutput").ap()

        DL = lambda n, s, t: nc.dram_tensor(n, s, t, kind="Internal").ap()
        hloc_d = DL("hloc_d", [16, 128, TOK], BF16)
        pp_d = DL("pp_d", [16, 128, TOK], BF16)
        g_d = DL("g_d", [16, 128, TOK], BF16)
        xq_d = DL("xq_d", [16, 128, TOK], BF16)
        x2_d = DL("x2_d", [TOK, D], F32)
        vecs_d = DL("vecs_d", [6, D], F32)

        DS = lambda n, s, t: nc.dram_tensor(n, s, t, kind="Internal",
                                            addr_space="Shared").ap()
        wq_attn = DS("wq_attn", [4, D, D], BF16)
        wq_gate = DS("wq_gate", [64, D, 128], BF16)
        wq_up = DS("wq_up", [64, D, 128], BF16)
        wq_down = DS("wq_down", [4, I8K, 512], BF16)
        mod_sh = DS("mod_sh", [2, 6 * D], F32)
        wsum_sh = DS("wsum_sh", [1, 32], F32)
        summ_sh = DS("summ_sh", [8, 128, 16], F32)
        flag_sh = DS("flag_sh", [1, 64], I32)

        es0 = es.enter_context(contextlib.ExitStack())
        esA = es.enter_context(contextlib.ExitStack())
        es1 = es.enter_context(contextlib.ExitStack())
        sb0 = lambda n, s, t: es0.enter_context(nc.sbuf_tensor(n, s, t))
        sbA = lambda n, s, t: esA.enter_context(nc.sbuf_tensor(n, s, t))
        sb1 = lambda n, s, t: es1.enter_context(nc.sbuf_tensor(n, s, t))

        mkscr = sbt("mkscr", [1, 8], F32)
        b.mkv = mkscr[0:1, 0:1]
        b.mks = mkscr[0:1, 2:3]
        identf = sbt("identf", [128, 128], F32)
        identb = sbt("identb", [128, 128], BF16)
        h128b = sbt("h128b", [128, 128], BF16)
        onesrow = sbt("onesrow", [1, 128], F32)
        onescol_b = sbt("onescol_b", [128, 1], BF16)
        onescol_f = sbt("onescol_f", [128, 1], F32)
        maskt = sbt("maskt", [128, 8], F32)
        rolet = sbt("rolet", [1, 8], I32)
        magict = sbt("magict", [1, 8], I32)
        fl_sb = sbt("fl_sb", [1, 64], I32)
        mwbc = sbt("mwbc", [128, 8], F32)
        phal = sbt("phal", [128, 8], F32)
        nhal = sbt("nhal", [128, 8], F32)
        col = sbt("col", [128, 48], F32)
        wrow = sbt("wrow", [1, 48], F32)
        wbuf = sbt("wbuf", [128, 4 * 1024], BF16)
        qs_i = sbt("qs_i", [128, D], I32)
        a_ffn = sbt("a_ffn", [1, D], F32)
        rep = {}
        rep["gb"] = sbt("rep_gb", [128, D], BF16)

        tern_b = sb0("tern_b", [128, D], F32)
        cvt_t = sb0("cvt_t", [2, D], F32)
        nrowp = sb0("nrowp", [97, D], F32)
        nrow = [nrowp[0:1, :], nrowp[32:33, :], nrowp[64:65, :], nrowp[96:97, :]]
        nrowp2 = sb0("nrowp2", [1, D], F32)
        nrow.append(nrowp2[0:1, :])
        wsrc = sb0("wsrc", [128, 4 * D], F32)
        wqt = sb0("wqt", [128, D], BF16)
        wsum_acc = sb0("wsum_acc", [128, 8], F32)
        modmine = sb0("modmine", [2, 6144], F32)
        modrp = sb0("modrp", [33, 6 * D], F32)
        modr0 = modrp[0:1, :]
        modr1 = modrp[32:33, :]
        vrowp = sb0("vrowp", [33, D], F32)
        vrow = vrowp[0:1, :]
        vrow2 = vrowp[32:33, :]
        vrowb = sb0("vrowb", [1, D], F32)[0:1, :]
        vrowc = sb0("vrowc", [1, D], F32)[0:1, :]
        scT = sb0("scT", [128, 32], F32)

        PS = [es.enter_context(nc.psum_tensor(f"PS{i}", [128, 512], F32))
              for i in range(8)]
        PSB = [PS[i][:].bitcast(BF16) for i in range(8)]

        grr = {}

        def gp_role(gp):
            if "r" not in grr:
                grr["r"] = gp.alloc_register(name="role_reg")
                gp.load(grr["r"], rolet[0:1, 0:1])
            return grr["r"]

        def poll_both(tag, wa, wb):
            s = b.sem("poll_" + tag)

            def th(gp):
                r0 = gp.alloc_register(name="pr0_" + tag)
                r1 = gp.alloc_register(name="pr1_" + tag)
                ni = gp.alloc_register(name="pni_" + tag)
                wr2 = gp.alloc_register(name="pwr_" + tag)
                gp.reg_mov(r0, 0); gp.reg_mov(r1, 0)
                gp.reg_mov(ni, 0); gp.reg_mov(wr2, 0)
                ENG = [ET.Pool]
                nc.br("pl_" + tag, engines=ENG)
                with nc.body("pl_" + tag, valid_engines=ENG):
                    gp.dma_start(out=fl_sb[:], in_=flag_sh).then_inc(s, 16)
                    gp.reg_add(ni, ni, 1)
                    gp.reg_alu(wr2, ni, 4, A.logical_shift_left)
                    gp.wait_ge(s, wr2)
                    gp.load(r0, fl_sb[0:1, wa:wa + 1])
                    gp.load(r1, fl_sb[0:1, wb:wb + 1])
                    gp.br_eq(r0, MAGIC, on_true="pc_" + tag, on_false="pl_" + tag)
                with nc.body("pc_" + tag, valid_engines=ENG):
                    gp.br_eq(r1, MAGIC, on_true="pd_" + tag, on_false="pl_" + tag)
                nc.switch_bb("pd_" + tag)
            raw("gp", th)

        def role_write(dst_a, dst_b, src, semname):
            h = b.sem(semname)
            b.bump(semname, 16)

            def th(gp):
                r = gp_role(gp)
                with gp.If_eq(r, 0):
                    gp.dma_start(out=dst_a, in_=src).then_inc(h, 16)
                with gp.Else():
                    gp.dma_start(out=dst_b, in_=src).then_inc(h, 16)
                gp.end_ifs()
            raw("gp", th)

        def set_flag(wa, semname):
            role_write(flag_sh[0:1, wa:wa + 1], flag_sh[0:1, wa + 1:wa + 2],
                       magict[0:1, 0:1], semname)

        def th_init(v):
            v.memset(onesrow[:], 1.0)
            v.memset(onescol_b[:], 1.0)
            v.memset(onescol_f[:], 1.0)
            v.memset(magict[:], MAGIC)
            v.memset(wsum_acc[:], 0.0)
            v.memset(wrow[:], 0.0)
        raw("v", th_init)
        op("v", "nop").inc("init_v", 1)

        dma("sy", identf[:], constf[:, 0:128], "ld0")
        dma("sy", maskt[:], maskv, "ld0")
        dma("sy", rolet[:], rolei, "ld0")
        dma("sy", cvt_t[:], cvec, "ld0")
        for i in range(5):
            dma("sy", nrow[i], norms5[i:i + 1, :], "ld0")
        dma("gp", identb[:], constf[:, 0:128], "ldg0")
        dma("gp", h128b[:], constf[:, 128:256], "ldg0")

        ws0 = b.val("wabs_s")
        wait("sc", "init_v")
        for t in range(4):
            eng = "sy" if t % 2 == 0 else "sc"
            dma(eng, wsrc[:, (t % 4) * D:(t % 4 + 1) * D],
                qsrc[t * 128:(t + 1) * 128, :], "wld" + str(t % 2))
        wd0 = b.val("wabs_done")
        for t in range(QTILES):
            wait("sc", "wld" + str(t % 2), (t // 2 + 1) * 16)
            if t >= 2:
                wait("sc", "wabs_done", wd0 + t - 1)
            op("sc", "activation", tern_b[:], wsrc[:, (t % 4) * D:(t % 4 + 1) * D],
               AF.Abs, bias=0.0, scale=1.0,
               accum_out=col[:, 44 + (t % 2):45 + (t % 2)]).inc("wabs_s", 1)
            tn = t + 4
            if tn < QTILES:
                eng = "sy" if tn % 2 == 0 else "sc"
                wait(eng, "wabs_s", ws0 + t + 1)
                dma(eng, wsrc[:, (tn % 4) * D:(tn % 4 + 1) * D],
                    qsrc[tn * 128:(tn + 1) * 128, :], "wld" + str(tn % 2))
        wd0 = b.val("wabs_done")
        for t in range(QTILES):
            wi = W_ORDER.index(TILE_W[t][0])
            wait("v", "wabs_s", ws0 + t + 1)
            op("v", "tensor_tensor", col[:, 32 + wi:33 + wi],
               wsum_acc[:, wi:wi + 1], col[:, 44 + (t % 2):45 + (t % 2)], A.add)
            op("v", "tensor_copy", wsum_acc[:, wi:wi + 1],
               col[:, 32 + wi:33 + wi]).inc("wabs_done", 1)

        wait("te", "wabs_done")
        op("te", "matmul", PS[0][0:1, 0:8], onescol_f[:], wsum_acc[:],
           start=True, stop=True).inc("wsum_mm", 1)
        wait("v", "wsum_mm")
        op("v", "tensor_copy", wrow[0:1, 0:8], PS[0][0:1, 0:8]).inc("wsum_row", 1)

        wait("gp", "wsum_row")
        role_write(wsum_sh[0:1, 0:8], wsum_sh[0:1, 8:16], wrow[0:1, 0:8], "wsum_wr")
        wait("gp", "wsum_wr")
        set_flag(0, "wsumf")
        wait("gp", "wsumf")
        poll_both("wsum", 0, 1)
        dma("gp", wrow[0:1, 16:32], wsum_sh[0:1, 0:16], "wsum_rd")
        wait("v", "wsum_rd")
        op("v", "tensor_tensor", wrow[0:1, 32:40], wrow[0:1, 16:24],
           wrow[0:1, 24:32], A.add)
        for wi, w in enumerate(W_ORDER):
            op("v", "tensor_scalar", wrow[0:1, 40 + wi:41 + wi],
               wrow[0:1, 32 + wi:33 + wi], 1.0 / W_NUMEL[w], CLIP_MIN, A.mult, A.max)
        op("v", "nop").inc("wmean", 1)
        wait("te", "wmean")
        op("te", "matmul", PS[0][:, 0:8], onesrow[:], wrow[0:1, 40:48],
           start=True, stop=True).inc("wmean_mm", 1)
        wait("v", "wmean_mm")
        op("v", "tensor_copy", mwbc[:], PS[0][:, 0:8])
        op("v", "tensor_scalar", phal[:], mwbc[:], 0.5, None, A.mult)
        op("v", "tensor_scalar", nhal[:], mwbc[:], -0.5, None, A.mult)
        op("v", "nop").inc("wthr", 1)

        wait("sy", "wthr")
        wait("sc", "wthr")
        wt0 = b.val("wq_tern")
        for t in range(4):
            eng = "sy" if t % 2 == 0 else "sc"
            dma(eng, wsrc[:, (t % 4) * D:(t % 4 + 1) * D],
                qsrc[t * 128:(t + 1) * 128, :], "wl2" + str(t % 2))
        for t in range(QTILES):
            wi = W_ORDER.index(TILE_W[t][0])
            wait("v", "wl2" + str(t % 2), (t // 2 + 1) * 16)
            if t >= 1:
                wait("v", "wq_wr", t * 16)
            src = wsrc[:, (t % 4) * D:(t % 4 + 1) * D]
            op("v", "tensor_scalar", tern_b[:], src, nhal[:, wi:wi + 1], None, A.is_le)
            op("v", "scalar_tensor_tensor", wqt[:], src, phal[:, wi:wi + 1],
               tern_b[:], A.is_ge, A.subtract).inc("wq_tern", 1)
            tn = t + 4
            if tn < QTILES:
                eng = "sy" if tn % 2 == 0 else "sc"
                wait(eng, "wq_tern", wt0 + t + 1)
                dma(eng, wsrc[:, (tn % 4) * D:(tn % 4 + 1) * D],
                    qsrc[tn * 128:(tn + 1) * 128, :], "wl2" + str(tn % 2))
        for t in range(QTILES):
            w, off = TILE_W[t]
            wait("gp", "wq_tern", wt0 + t + 1)
            h = b.sem("wq_wr")
            b.bump("wq_wr", 16)

            def th(gp, w=w, off=off, h=h):
                r = gp_role(gp)
                if w in ("wi", "wf", "wg", "wo"):
                    widx = W_ORDER.index(w)
                    with gp.If_eq(r, 0):
                        gp.dma_start(out=wq_attn[widx, off:off + 128, :],
                                     in_=wqt[:]).then_inc(h, 16)
                    with gp.Else():
                        gp.dma_start(out=wq_attn[widx, 1024 + off:1024 + off + 128, :],
                                     in_=wqt[:]).then_inc(h, 16)
                    gp.end_ifs()
                elif w in ("wgate", "wup"):
                    pool = wq_gate if w == "wgate" else wq_up
                    dv = pool.rearrange("a d c -> d a c")
                    srcv = wqt[:].rearrange("d (j c) -> d j c", c=128)
                    s_seg, dloc = off // 1024, off % 1024
                    asl = slice(s_seg * 16, (s_seg + 1) * 16)
                    with gp.If_eq(r, 0):
                        gp.dma_start(out=dv[dloc:dloc + 128, asl, :],
                                     in_=srcv).then_inc(h, 16)
                    with gp.Else():
                        gp.dma_start(out=dv[1024 + dloc:1024 + dloc + 128, asl, :],
                                     in_=srcv).then_inc(h, 16)
                    gp.end_ifs()
                else:
                    dv = wq_down.rearrange("a d c -> d a c")
                    srcv = wqt[:].rearrange("d (a c) -> d a c", a=4)
                    with gp.If_eq(r, 0):
                        gp.dma_start(out=dv[off:off + 128, :, :],
                                     in_=srcv).then_inc(h, 16)
                    with gp.Else():
                        gp.dma_start(out=dv[4096 + off:4096 + off + 128, :, :],
                                     in_=srcv).then_inc(h, 16)
                    gp.end_ifs()
            raw("gp", th)
        wait("gp", "wq_wr")
        set_flag(4, "wqf")
        wait("gp", "wqf")
        poll_both("wq", 4, 5)
        op("gp", "nop").inc("wq_ready", 1)

        wait("sc", "ld0", 16 * 4)
        op("sc", "activation", cvt_t[:], cvt_t[:], AF.Silu,
           bias=0.0, scale=1.0).inc("silu_c", 1)
        wait("te", "silu_c")
        wait("te", "ld0", 16)

        hk_sct = b.tinc("scT_mm")

        def th_sct(te, hk=hk_sct):
            for kc in range(16):
                instr = te.transpose(PS[1][:, kc * 2:kc * 2 + 2],
                                     cvt_t[:, kc * 128:(kc + 1) * 128], identf[0:2, 0:2])
                if kc == 15:
                    instr.then_inc(*hk)
        raw("te", th_sct)
        wait("v", "scT_mm")
        op("v", "tensor_copy", scT[:, 0:32], PS[1][:, 0:32]).inc("scT_ev", 1)

        wait("sc", "scT_ev")
        mmM = b.val("modmm")
        me0 = b.val("mod_ev")
        for idx in range(12 * 16):
            if idx >= 4:
                wait("sc", "modmm", mmM + idx - 3)
            oc, kc = divmod(idx, 16)
            dma("sc", wbuf[:, (idx % 4) * 1024:(idx % 4) * 1024 + 1024]
                .bitcast(F32)[:, 0:512],
                wmod_sl[kc * 128:(kc + 1) * 128, oc * 512:(oc + 1) * 512], "modw_ld")
        for idx in range(12 * 16):
            oc, kc = divmod(idx, 16)
            wait("te", "modw_ld", (idx + 1) * 16)
            if kc == 0 and oc >= 1:
                wait("te", "mod_ev", me0 + oc)
            op("te", "matmul", PS[2][0:2, :],
               scT[:, kc * 2:kc * 2 + 2],
               wbuf[:, (idx % 4) * 1024:(idx % 4) * 1024 + 1024]
               .bitcast(F32)[:, 0:512],
               start=(kc == 0), stop=(kc == 15)).inc("modmm", 1)
        for oc in range(12):
            wait("v", "modmm", mmM + (oc + 1) * 16)
            op("v", "tensor_copy", modmine[:, oc * 512:(oc + 1) * 512],
               PS[2][0:2, :]).inc("mod_ev", 1)
        wait("gp", "mod_ev")
        role_write(mod_sh[:, 0:6144], mod_sh[:, 6144:12288], modmine[:], "mod_wr")
        wait("gp", "mod_wr")
        set_flag(2, "modf")
        wait("gp", "modf")
        poll_both("mod", 2, 3)
        dma("gp", modr0, mod_sh[0:1, :], "mod_rd")
        dma("gp", modr1, mod_sh[1:2, :], "mod_rd")

        wait("v", "mod_rd")
        wait("v", "ld0")
        MODN = ["sa", "ca1", "ga", "sb_", "cb1", "gb"]

        def stage_row(nm, j):
            op("v", "tensor_scalar", vrow, modr0[:, j * D:(j + 1) * D],
               maskt[0:1, 3:4], None, A.mult)
            op("v", "tensor_scalar", vrow2, modr1[:, j * D:(j + 1) * D],
               maskt[32:33, 4:5], None, A.mult)
            op("v", "tensor_copy", vrowb, vrow2)
            op("v", "tensor_tensor", vrow, vrow, vrowb, A.add)
            if nm in ("ca1", "cb1"):
                op("v", "tensor_scalar", vrowb, vrow, 1.0, None, A.add)
            elif nm in ("ga", "gb"):
                op("v", "tensor_scalar", vrowb, vrow, 0.0, None, A.is_ge)
                op("v", "tensor_scalar", vrowb, vrowb, 2.0, -1.0, A.mult, A.add)
                op("v", "tensor_scalar", vrowc, vrow, -1.0, None, A.mult)
                op("v", "tensor_tensor", vrowc, vrowc, vrow, A.max)
                op("v", "tensor_scalar", vrowc, vrowc, 0.25, None, A.max)
                op("v", "tensor_tensor", vrowb, vrowc, vrowb, A.mult)
            else:
                op("v", "tensor_copy", vrowb, vrow)
            op("v", "nop").inc("vst", 1)
            wait("gp", "vst")
            dma("gp", vecs_d[j:j + 1, :], vrowb, "vec_wr")
            wait("v", "vec_wr")

        for j, nm in enumerate(MODN):
            stage_row(nm, j)
        op("v", "nop").inc("vecs_done", 1)


        env = dict(locals())
        es0.close()
        _alloc_main(b, env)
        if STAGE >= 2:
            _p1_preamble(b, env)
            _phase1(b, env)
        esA.close()
        env["es3a"] = es.enter_context(contextlib.ExitStack())
        _alloc_3a(b, env)
        if STAGE >= 3:
            _p3a_preamble(b, env)
            _phase3a(b, env)
        if STAGE >= 4:
            _phase3b(b, env)
        _finish(b, env)

        blk = es.enter_context(nc.Block())
        b.finalize(blk)
    return nc, dbg


def _mk(env):
    b = env["b"]
    return (b, b.op, b.wait, b.dma, b.raw)


def _quant_tile(env, src, normrep, rstd_col, cm, dst):
    b, op, wait, dma, raw = _mk(env)
    col, qs_u, qs_y, qs_i, qs_t = (env[k] for k in
                                   ("col", "qs_u", "qs_y", "qs_i", "qs_t"))
    if normrep is not None:
        op("v", "tensor_tensor", qs_u[:], src, normrep, A.mult)
    else:
        op("v", "tensor_copy", qs_u[:], src)
    op("v", "tensor_reduce", col[:, 40:41], qs_u[:], AX, A.max,
       apply_absolute_value=True)
    if rstd_col is not None:
        op("v", "tensor_scalar", col[:, cm:cm + 1], col[:, 40:41],
           rstd_col, CLIP_MIN, A.mult, A.max)
    else:
        op("v", "tensor_scalar", col[:, cm:cm + 1], col[:, 40:41],
           1.0, CLIP_MIN, A.mult, A.max)
    op("v", "reciprocal", col[:, 41:42], col[:, cm:cm + 1])
    if rstd_col is not None:
        op("v", "tensor_scalar", col[:, 42:43], col[:, 41:42],
           rstd_col, 127.0, A.mult, A.mult)
    else:
        op("v", "tensor_scalar", col[:, 42:43], col[:, 41:42],
           127.0, None, A.mult)
    op("v", "tensor_scalar", qs_y[:], qs_u[:], col[:, 42:43],
       None, A.mult).inc("q_y", 1)
    wait("sc", "q_y")
    op("sc", "copy", qs_i[:], qs_y[:]).inc("q_cvt", 1)
    wait("v", "q_cvt")
    op("v", "tensor_tensor", qs_u[:], qs_y[:], qs_i[:], A.subtract)
    op("v", "tensor_scalar", qs_t[:], qs_u[:], 0.5, None, A.is_gt)
    op("v", "tensor_tensor", qs_y[:], qs_i[:], qs_t[:], A.add)
    op("v", "scalar_tensor_tensor", dst, qs_u[:], -0.5, qs_y[:],
       A.is_lt, A.subtract).inc("q_done", 1)


def _transpose_to_fm(env, src_bf, qT, j):
    b, op, wait, dma, raw = _mk(env)
    PSB, identb = env["PSB"], env["identb"]
    wait("te", "q_done", b.val("q_done"))
    for half in range(2):
        bank = 6 + half
        if b.val(f"tpe{bank}") > 0:
            wait("te", f"tpe{bank}")

        hk = b.tinc(f"tpm{bank}")

        def th(te, src_bf=src_bf, half=half, bank=bank, hk=hk):
            for k8 in range(8):
                kc = half * 8 + k8
                instr = te.transpose(PSB[bank][:, k8 * 128:(k8 + 1) * 128],
                                     src_bf[:, kc * 128:(kc + 1) * 128], identb[:])
                if k8 == 7:
                    instr.then_inc(*hk)
        raw("te", th)
        wait("v", f"tpm{bank}")
        dst = qT[:, :].rearrange("p (k r) -> p k r", r=MT)[:, half * 8:half * 8 + 8, j * 128:(j + 1) * 128]
        srcv = PSB[bank][:, 0:1024].rearrange("p (k r) -> p k r", r=128)
        op("v", "tensor_copy", dst, srcv).inc(f"tpe{bank}", 1)


def _col_to_row(env, cols, dstrow, tag):
    b, op, wait, dma, raw = _mk(env)
    PS, identf = env["PS"], env["identf"]
    b.vbar("te")

    hk = b.tinc("cr_t")

    def th(te, cols=cols, hk=hk):
        for c, ap in enumerate(cols):
            instr = te.transpose(PS[5][0:1, c * 128:(c + 1) * 128], ap, identf[:])
            if c == len(cols) - 1:
                instr.then_inc(*hk)
    raw("te", th)
    wait("v", "cr_t")
    op("v", "tensor_copy", dstrow[0:1, 0:len(cols) * 128],
       PS[5][0:1, 0:len(cols) * 128]).inc("cr_r", 1)
    env["b"].wait("te", "cr_r")


def _row_to_col(env, srcrow, dstcols, tag):
    b, op, wait, dma, raw = _mk(env)
    PS, identf = env["PS"], env["identf"]
    b.vbar("te")

    hk = b.tinc("rc_t")

    def th(te, n=len(dstcols), hk=hk):
        for c in range(n):
            instr = te.transpose(PS[5][:, c:c + 1],
                                 srcrow[0:1, c * 128:(c + 1) * 128], identf[0:1, 0:1])
            if c == n - 1:
                instr.then_inc(*hk)
    raw("te", th)
    wait("v", "rc_t")
    for c, ap in enumerate(dstcols):
        op("v", "tensor_copy", ap, PS[5][:, c:c + 1])
    op("v", "nop").inc("rc_r", 1)
    env["b"].wait("te", "rc_r")


def _bcast_row(env, srcrow, dstrep, tag, nchunk=1):
    b, op, wait, dma, raw = _mk(env)
    PS, onesrow = env["PS"], env["onesrow"]
    b.vbar("te")
    for c in range(nchunk):
        bk = 4 + (c % 2)
        if c >= 2:
            wait("te", f"bcbe{c % 2}", b.val(f"bcbe{c % 2}"))
        op("te", "matmul", PS[bk][:, 0:512], onesrow[:],
           srcrow[0:1, c * 512:(c + 1) * 512],
           start=True, stop=True).inc(f"bcbm{c}", 1)
        wait("v", f"bcbm{c}")
        op("v", "tensor_copy", dstrep[:, c * 512:(c + 1) * 512],
           PS[bk][:, 0:512]).inc(f"bcbe{c % 2}", 1)
    env["b"].wait("te", "bcbe0")
    if nchunk >= 2:
        env["b"].wait("te", "bcbe1")


def _sweep_attn(env, qT, wname, mtag, evac):
    b, op, wait, dma, raw = _mk(env)
    PS, wbuf, wq_attn = env["PS"], env["wbuf"], env["wq_attn"]
    widx = W_ORDER.index(wname)
    wl0 = b.val(mtag + "_wl")
    mm0 = b.val(mtag + "_mm")
    ev0 = b.val(mtag + "_ev")
    prev = env.get("_sweep_prev")
    if prev is not None:
        wait("te", prev[0], prev[1])
    wait("te", "tpe6", b.val("tpe6"))
    wait("te", "tpe7", b.val("tpe7"))
    mm_tick = {}
    for ci in range(32):
        oh, kc = divmod(ci, 16)
        if ci >= 4:
            wait("sy", mtag + "_mm", mm_tick[ci - 4])
        else:
            wait("sy", mtag + "_mm", b.val(mtag + "_mm"))
        dma("sy", wbuf[:, (ci % 4) * 1024:(ci % 4 + 1) * 1024],
            wq_attn[widx, kc * 128:(kc + 1) * 128, oh * 1024:(oh + 1) * 1024],
            mtag + "_wl")
        wait("te", mtag + "_wl", wl0 + (ci + 1) * 16)
        if kc == 0 and oh == 1:
            wait("te", mtag + "_ev", ev0 + 8)

        hk = b.tinc(mtag + "_mm")

        def th(te, ci=ci, kc=kc, hk=hk):
            for oc8 in range(8):
                instr = te.matmul(PS[oc8][:, 0:512],
                                  wbuf[:, (ci % 4) * 1024 + oc8 * 128:(ci % 4) * 1024 + (oc8 + 1) * 128],
                                  qT[:, kc * MT:(kc + 1) * MT],
                                  start=(kc == 0), stop=(kc == 15))
                if oc8 == 7:
                    instr.then_inc(*hk)
        raw("te", th)
        mm_tick[ci] = b.val(mtag + "_mm")
        if kc == 15:
            for oc8 in range(8):
                oc = oh * 8 + oc8
                wait("v", mtag + "_mm", mm_tick[ci])
                evac(oc, oc8)
    env["_sweep_prev"] = (mtag + "_ev", b.val(mtag + "_ev"))


def _alloc_main(b, env):
    nc, es1, esA = env["nc"], env["es1"], env["esA"]
    s1 = lambda n, s, t: es1.enter_context(nc.sbuf_tensor(n, s, t))
    sA = lambda n, s, t: esA.enter_context(nc.sbuf_tensor(n, s, t))
    env["ch"] = s1("ch", [128, 16], F32)
    env["cP"] = s1("cP", [128, 16], F32)
    env["carry"] = s1("carry", [128, 16], F32)
    env["sm16a"] = s1("sm16a", [128, 16], F32)
    env["sm16b"] = s1("sm16b", [128, 16], F32)
    env["summ_all"] = s1("summ_all", [128, 8 * 16], F32)
    env["ones16"] = s1("ones16", [128, 16], F32)
    env["qs_t"] = s1("qs_t", [128, D], F32)
    env["qtm"] = s1("qtm", [128, D], BF16)
    env["bigA"] = s1("bigA", [128, 4 * D], F32)
    env["tsc"] = s1("tsc", [128, D], F32)
    env["qT_a"] = s1("qT_a", [128, 16 * MT], BF16)
    env["st_a"] = s1("st_a", [128, MT], F32)
    env["st_b"] = s1("st_b", [128, MT], F32)
    env["st_c"] = s1("st_c", [128, MT], F32)
    env["st_d"] = s1("st_d", [128, MT], F32)
    env["st_e"] = s1("st_e", [128, MT], F32)
    env["st_f"] = s1("st_f", [128, MT], F32)
    env["arow"] = s1("arow", [1, 512], F32)
    env["arow2"] = s1("arow2", [1, 512], F32)
    env["brep"] = s1("brep", [128, MT], F32)
    env["brep2"] = s1("brep2", [128, MT], F32)
    sg = s1("sgpack", [65, D], F32)
    env["ssg"] = sg[0:1, :]
    env["a_go"] = sg[64:65, :]
    env["qs_u"] = sA("qs_u", [128, D], F32)
    env["qs_y"] = sA("qs_y", [128, D], F32)
    env["qT_b"] = sA("qT_b", [128, 16 * MT], BF16)
    env["iraw"] = sA("iraw", [128, 16 * MT], BF16)
    env["sp_bf"] = sA("sp_bf", [128, 8 * MT], BF16)
    env["zeros512"] = sA("zeros512", [128, 512], F32)
    for nm in ["ca1", "sa", "n_i", "n_f", "n_g"]:
        env["rep"][nm] = sA("rep_" + nm, [128, D], BF16)
    op = b.op
    def th(v):
        v.memset(env["ch"][:], 0.0)
        v.memset(env["cP"][:], 1.0)
        v.memset(env["ones16"][:], 1.0)
        v.memset(env["zeros512"][:], 0.0)
    b.raw("v", th)
    op("v", "nop").inc("alloc_init", 1)


def _alloc_3a(b, env):
    nc, es3a = env["nc"], env["es3a"]
    s3 = lambda n, s, t: es3a.enter_context(nc.sbuf_tensor(n, s, t))
    env["yw"] = s3("yw", [128, 16 * MT], BF16)
    env["yw2"] = s3("yw2", [128, 16 * MT], BF16)
    env["gnw_fm"] = s3("gnw_fm", [128, 16], F32)
    env["no_fm"] = s3("no_fm", [128, 16], F32)
    env["gst"] = s3("gst", [128, 2 * MT], BF16)
    for nm in ["cb1", "sb_", "ga"]:
        env["rep"][nm] = s3("rep_" + nm, [128, D], BF16)


def _replicate_from_row(env, rowsrc_dram, nm, tag):
    b = env["b"]
    op, wait, dma, raw = b.op, b.wait, b.dma, b.raw
    qs_t, PS, onesrow, rep = env["qs_t"], env["PS"], env["onesrow"], env["rep"]
    row = qs_t[0:1, :]
    wait("sy", "rp_free", b.val("rp_free"))
    dma("sy", row, rowsrc_dram, "rp_ld")
    wait("te", "rp_ld")
    for c in range(4):
        bk = 3 + (c % 2)
        if c >= 2:
            wait("te", f"rpe{c % 2}", b.val(f"rpe{c % 2}"))
        op("te", "matmul", PS[bk][:, 0:512], onesrow[:],
           row[:, c * 512:(c + 1) * 512], start=True, stop=True).inc(f"rpm{c}", 1)
        wait("v", f"rpm{c}")
        op("v", "tensor_copy", rep[nm][:, c * 512:(c + 1) * 512],
           PS[bk][:, 0:512]).inc(f"rpe{c % 2}", 1)
    op("v", "nop").inc("rp_free", 1)


def _p1_preamble(b, env):
    op, wait = b.op, b.wait
    wait("v", "vecs_done")
    wait("v", "alloc_init")
    wait("sy", "vecs_done")
    norms5, vecs_d = env["norms5"], env["vecs_d"]
    _replicate_from_row(env, vecs_d[1:2, :], "ca1", "rpca1")
    _replicate_from_row(env, vecs_d[0:1, :], "sa", "rpsa")
    _replicate_from_row(env, norms5[0:1, :], "n_i", "rpni")
    _replicate_from_row(env, norms5[1:2, :], "n_f", "rpnf")
    _replicate_from_row(env, norms5[2:3, :], "n_g", "rpng")
    op("v", "nop").inc("reps_done", 1)


def _p3a_preamble(b, env):
    op, wait, dma, raw = b.op, b.wait, b.dma, b.raw
    norms5, vecs_d = env["norms5"], env["vecs_d"]
    qs_t, PS, identf = env["qs_t"], env["PS"], env["identf"]
    _replicate_from_row(env, vecs_d[4:5, :], "cb1", "rpcb1")
    _replicate_from_row(env, vecs_d[3:4, :], "sb_", "rpsb")
    _replicate_from_row(env, vecs_d[2:3, :], "ga", "rpga")
    _replicate_from_row(env, vecs_d[5:6, :], "gb", "rpgb")
    for srcd, dstn, tag in ((norms5[4:5, :], "gnw_fm", "gnwb"),
                            (norms5[3:4, :], "no_fm", "nofb")):
        row = qs_t[0:1, :]
        wait("sy", "nf_free", b.val("nf_free"))
        dma("sy", row, srcd, "nf_ld")
        wait("te", "nf_ld")

        hk = b.tinc("nf_mm")

        def th(te, row=row, hk=hk):
            for s_ in range(16):
                instr = te.transpose(PS[5][:, s_:s_ + 1],
                                     row[:, s_ * 128:(s_ + 1) * 128], identf[0:1, 0:1])
                if s_ == 15:
                    instr.then_inc(*hk)
        raw("te", th)
        wait("v", "nf_mm")
        op("v", "tensor_copy", env[dstn][:], PS[5][:, 0:16]).inc(tag + "_ev", 1)
        wait("te", tag + "_ev")
        op("v", "nop").inc("nf_free", 1)
    op("v", "nop").inc("reps3a_done", 1)
    b.wait("te", "reps3a_done")


def _phase1(b, env):
    op, wait, dma, raw = b.op, b.wait, b.dma, b.raw
    g = env.__getitem__
    (xo, xa, bigA, tsc, col, qs_t, rep, qT_a, qT_b, qtm, iraw, brep, brep2,
     st_a, st_b, st_c, st_d, st_e, ch, cP, zeros512, sp_bf, hloc_d, pp_d, g_d,
     arow, arow2, mwbc, a_go, ssg, PS, onescol_b, sm16a, sm16b, maskt, ones16,
     summ_sh, summ_all, carry, magict) = (g(k) for k in (
        "xo", "xa", "bigA", "tsc", "col", "qs_t", "rep", "qT_a", "qT_b", "qtm",
        "iraw", "brep", "brep2", "st_a", "st_b", "st_c", "st_d", "st_e", "ch",
        "cP", "zeros512", "sp_bf", "hloc_d", "pp_d", "g_d", "arow", "arow2",
        "mwbc", "a_go", "ssg", "PS", "onescol_b", "sm16a", "sm16b", "maskt",
        "ones16", "summ_sh", "summ_all", "carry", "magict"))
    role_write, set_flag, poll_both = (env[k] for k in
                                       ("role_write", "set_flag", "poll_both"))

    wait("v", "reps_done")
    wait("sy", "reps_done")
    wait("sy", "wq_ready")
    wait("v", "wq_ready")
    wait("te", "wq_ready")
    wait("te", "reps_done")
    wait("sc", "wq_ready")

    spslot = [0]

    def spill(srcap, dst_dram, cast_bf=True):
        pidx = spslot[0]
        spslot[0] += 1
        sl = pidx % 8
        if pidx >= 8:
            wait("v", "sp_wr", 16 * (pidx - 7))
        op("v", "tensor_copy", sp_bf[:, sl * MT:(sl + 1) * MT],
           srcap).inc("sp_cp", 1)
        wait("gp", "sp_cp")
        dma("gp", dst_dram, sp_bf[:, sl * MT:(sl + 1) * MT], "sp_wr")

    def emit_mt(m, aux):
        xsrc = xa if aux else xo
        if b.val("p1fin") > 0:
            wait("sy", "p1fin")
        dma("sy", bigA[:].rearrange("p (j d) -> p j d", d=D),
            xsrc[m * MT:(m + 1) * MT, :].rearrange("(j p) d -> p j d", p=128),
            "x_ld")
        wait("v", "x_ld")
        wait("sc", "x_ld")
        for j in range(4):
            xj = bigA[:, j * D:(j + 1) * D]
            op("sc", "activation", qs_t[:], xj, AF.Square, bias=0.0, scale=1.0,
               accum_out=col[:, 0:1]).inc("ln_sq", 1)
            op("v", "tensor_reduce", col[:, 1:2], xj, AX, A.add)
            wait("v", "ln_sq")
            op("v", "tensor_scalar", col[:, 2:3], col[:, 1:2], 1.0 / D, None, A.mult)
            op("v", "tensor_scalar", col[:, 3:4], col[:, 0:1], 1.0 / D, None, A.mult)
            op("v", "tensor_scalar", col[:, 4:5], col[:, 2:3], col[:, 2:3], None, A.mult)
            op("v", "tensor_tensor", col[:, 5:6], col[:, 3:4], col[:, 4:5], A.subtract)
            op("v", "tensor_scalar", col[:, 6:7], col[:, 5:6], EPS_LN, None,
               A.add).inc("ln_v", 1)
            wait("sc", "ln_v")
            op("sc", "activation", col[:, 7:8], col[:, 6:7], AF.Sqrt,
               bias=0.0, scale=1.0).inc("ln_sqrt", 1)
            wait("v", "ln_sqrt")
            op("v", "reciprocal", col[:, 8:9], col[:, 7:8])
            op("v", "tensor_scalar", col[:, 9:10], col[:, 2:3], col[:, 8:9],
               -1.0, A.mult, A.mult).inc("ln_ab", 1)
            wait("sc", "ln_ab")
            op("sc", "activation", tsc[:], xj, AF.Identity, bias=col[:, 9:10],
               scale=col[:, 8:9]).inc("ln_t", 1)
            wait("v", "ln_t")
            op("v", "tensor_tensor", qs_t[:], tsc[:], rep["ca1"][:], A.mult)
            op("v", "tensor_tensor", xj, qs_t[:], rep["sa"][:],
               A.add).inc("h1", 1)
            wait("sc", "h1")
            op("sc", "activation", qs_t[:], xj, AF.Square, bias=0.0, scale=1.0,
               accum_out=col[:, 44:45]).inc("rms_sq", 1)
            wait("v", "rms_sq")
            op("v", "tensor_scalar", col[:, 45:46], col[:, 44:45], 1.0 / D,
               EPS_BL, A.mult, A.add).inc("rms_v", 1)
            wait("sc", "rms_v")
            op("sc", "activation", col[:, 46:47], col[:, 45:46], AF.Sqrt,
               bias=0.0, scale=1.0).inc("rms_sqrt", 1)
            wait("v", "rms_sqrt")
            op("v", "reciprocal", col[:, 10 + j:11 + j], col[:, 46:47])
        for qT, nr, cmbase in ((qT_a, "n_i", 16), (qT_b, "n_f", 20)):
            for j in range(4):
                _quant_tile(env, bigA[:, j * D:(j + 1) * D], rep[nr][:],
                            col[:, 10 + j:11 + j], cmbase + j, qtm[:])
                _transpose_to_fm(env, qtm[:], qT, j)
        for proj, cmbase, wname, rowdst in (("i", 16, "wi", arow),
                                            ("f", 20, "wf", arow2)):
            wi = W_ORDER.index(wname)
            for j in range(4):
                op("v", "tensor_scalar", col[:, 24 + j:25 + j],
                   col[:, cmbase + j:cmbase + 1 + j], mwbc[:, wi:wi + 1],
                   -1.0 / 127.0, A.mult, A.mult)
            op("v", "nop").inc(f"ar{proj}_c", 1)
            _col_to_row(env, [col[:, 24 + j:25 + j] for j in range(4)],
                        rowdst, f"ar{proj}")
        _bcast_row(env, arow, brep, "bci")
        _bcast_row(env, arow2, brep2, "bcf")

        def evac_i(oc, bank):
            op("v", "tensor_copy", iraw[:, oc * MT:(oc + 1) * MT],
               PS[bank][:, 0:512]).inc("swi_ev", 1)
        _sweep_attn(env, qT_a, "wi", "swi", evac_i)

        def evac_f(oc, bank):
            op("v", "tensor_tensor", st_a[:], iraw[:, oc * MT:(oc + 1) * MT],
               brep[:], A.mult).inc("ep_it", 1)
            wait("sc", "ep_it")
            op("sc", "activation", st_b[:], st_a[:], AF.Silu,
               bias=0.0, scale=1.0).inc("ep_si", 1)
            op("v", "tensor_tensor", st_c[:], PS[bank][:, 0:512], brep2[:],
               A.mult).inc("swf_ev", 1)
            op("v", "nop").inc("ep_ft", 1)
            wait("sc", "ep_ft")
            op("sc", "activation", st_d[:], st_c[:], AF.Sigmoid,
               bias=0.0, scale=1.0).inc("ep_fg", 1)
            wait("v", "ep_fg")
            wait("v", "ep_si")
            op("v", "tensor_scalar", st_e[:], st_d[:], -1.0, 1.0, A.mult, A.add)
            op("v", "tensor_tensor", st_c[:], st_b[:], st_e[:], A.mult)
            op("v", "tensor_tensor_scan", st_b[:], st_d[:], st_c[:],
               ch[:, oc:oc + 1], A.mult, A.add)
            op("v", "tensor_copy", ch[:, oc:oc + 1], st_b[:, MT - 1:MT])
            op("v", "tensor_tensor_scan", st_e[:], st_d[:], zeros512[:],
               cP[:, oc:oc + 1], A.mult, A.add)
            op("v", "tensor_copy", cP[:, oc:oc + 1], st_e[:, MT - 1:MT])
            if not aux:
                spill(st_b[:], hloc_d[oc, :, m * MT:(m + 1) * MT])
                spill(st_e[:], pp_d[oc, :, m * MT:(m + 1) * MT])
        _sweep_attn(env, qT_b, "wf", "swf", evac_f)
        if DEBUG_DUMPS and m == 0 and not aux:
            nc = env["nc"]
            d_ir = nc.dram_tensor("dbg_iraw", [128, 16 * MT], BF16,
                                  kind="ExternalOutput").ap()
            d_b1 = nc.dram_tensor("dbg_brep", [128, MT], F32,
                                  kind="ExternalOutput").ap()
            d_b2 = nc.dram_tensor("dbg_brep2", [128, MT], F32,
                                  kind="ExternalOutput").ap()
            b.vbar("gp")
            dma("gp", d_ir, iraw[:], "irdmp")
            dma("gp", d_b1, brep[:], "irdmp")
            dma("gp", d_b2, brep2[:], "irdmp")
            wait("v", "irdmp")

        if not aux:
            wait("v", "swi_ev", b.val("swi_ev"))
            for j in range(4):
                _quant_tile(env, bigA[:, j * D:(j + 1) * D], rep["n_g"][:],
                            col[:, 10 + j:11 + j], 28 + j, qtm[:])
                _transpose_to_fm(env, qtm[:], qT_a, j)
            wig = W_ORDER.index("wg")
            for j in range(4):
                op("v", "tensor_scalar", col[:, 24 + j:25 + j],
                   col[:, 28 + j:29 + j], mwbc[:, wig:wig + 1],
                   1.0 / 127.0, A.mult, A.mult)
            op("v", "nop").inc("arg_c", 1)
            _col_to_row(env, [col[:, 24 + j:25 + j] for j in range(4)],
                        arow, "arg")
            op("v", "tensor_copy", a_go[:, m * MT:(m + 1) * MT],
               arow[0:1, 0:512])

            def evac_g(oc, bank):
                pidx = spslot[0]
                spslot[0] += 1
                sl = pidx % 8
                if pidx >= 8:
                    wait("v", "sp_wr", 16 * (pidx - 7))
                gsl = sp_bf[:, sl * MT:(sl + 1) * MT]
                op("v", "tensor_copy", gsl, PS[bank][:, 0:512]).inc("gev", 1)
                wait("sc", "gev")
                op("sc", "activation", qtm[:, 0:512], gsl, AF.Square,
                   bias=0.0, scale=1.0).inc("gsq", 1)
                wait("te", "gsq")
                op("te", "matmul", PS[bank][0:1, 0:512], onescol_b[:],
                   qtm[:, 0:512], start=True, stop=True).inc("gsmm", 1)
                wait("v", "gsmm")
                if oc == 0:
                    op("v", "tensor_copy", arow[0:1, 0:512],
                       PS[bank][0:1, 0:512]).inc("swg_ev", 1)
                else:
                    op("v", "tensor_copy", arow2[0:1, 0:512], PS[bank][0:1, 0:512])
                    op("v", "tensor_tensor", arow[0:1, 0:512], arow[0:1, 0:512],
                       arow2[0:1, 0:512], A.add).inc("swg_ev", 1)
                op("v", "nop").inc("gspc", 1)
                wait("gp", "gspc")
                dma("gp", g_d[oc, :, m * MT:(m + 1) * MT], gsl, "sp_wr")
            _sweep_attn(env, qT_a, "wg", "swg", evac_g)
            op("v", "tensor_copy", ssg[:, m * MT:(m + 1) * MT],
               arow[0:1, 0:512]).inc("g_done", 1)
        wait("gp", "sp_wr")
        op("v", "nop").inc("p1fin", 1)

    for m in range(NMT):
        emit_mt(m, aux=False)
    op("v", "tensor_copy", sm16a[:], cP[:])
    op("v", "tensor_copy", sm16b[:], ch[:]).inc("own_summ", 1)
    wait("gp", "own_summ")
    role_write(summ_sh[4], summ_sh[5], sm16a[:], "osum_wr")
    role_write(summ_sh[6], summ_sh[7], sm16b[:], "osum_wr")
    wait("v", "osum_wr")
    op("v", "memset", ch[:], 0.0)
    op("v", "memset", cP[:], 1.0)
    for m in range(NMT):
        emit_mt(m, aux=True)
    op("v", "tensor_scalar", sm16a[:], cP[:], maskt[:, 0:1], None, A.mult)
    op("v", "tensor_scalar", col[:, 0:1], maskt[:, 0:1], -1.0, 1.0, A.mult, A.add)
    op("v", "tensor_scalar", sm16b[:], ones16[:], col[:, 0:1], None, A.mult)
    op("v", "tensor_tensor", sm16a[:], sm16a[:], sm16b[:], A.add)
    op("v", "tensor_scalar", sm16b[:], ch[:], maskt[:, 0:1], None,
       A.mult).inc("aux_summ", 1)
    wait("gp", "aux_summ")
    role_write(summ_sh[0], summ_sh[1], sm16a[:], "asum_wr")
    role_write(summ_sh[2], summ_sh[3], sm16b[:], "asum_wr")
    wait("gp", "asum_wr")
    set_flag(6, "summf")
    wait("gp", "summf")
    poll_both("summ", 6, 7)
    dma("gp", summ_all[:].rearrange("p (s c) -> p s c", c=16),
        summ_sh.rearrange("s p c -> p s c"), "summ_rd")
    wait("v", "summ_rd")
    SLOT = lambda s: summ_all[:, s * 16:(s + 1) * 16]
    op("v", "tensor_tensor", sm16a[:], SLOT(2), SLOT(1), A.mult)
    op("v", "tensor_tensor", sm16a[:], sm16a[:], SLOT(3), A.add)
    op("v", "tensor_tensor", sm16b[:], sm16a[:], SLOT(4), A.mult)
    op("v", "tensor_tensor", sm16b[:], sm16b[:], SLOT(6), A.add)
    op("v", "tensor_scalar", carry[:], sm16a[:], maskt[:, 1:2], None, A.mult)
    op("v", "scalar_tensor_tensor", carry[:], sm16b[:], maskt[:, 2:3],
       carry[:], A.mult, A.add)
    op("v", "nop").inc("carry_done", 1)


def _phase3a(b, env):
    op, wait, dma, raw = b.op, b.wait, b.dma, b.raw
    g = env.__getitem__
    (xo, bigA, tsc, col, rep, qT_a, qtm, qs_t, brep, brep2, st_a, st_b, st_c,
     st_d, st_e, st_f, yw, yw2, gst, carry, gnw_fm, no_fm, arow, arow2, mwbc,
     a_go, ssg, a_ffn, PS, PSB, onescol_b, identf, h128b, hloc_d, pp_d, g_d,
     xq_d, x2_d, qs_i) = (g(k) for k in (
        "xo", "bigA", "tsc", "col", "rep", "qT_a", "qtm", "qs_t", "brep",
        "brep2", "st_a", "st_b", "st_c", "st_d", "st_e", "st_f", "yw", "yw2",
        "gst", "carry", "gnw_fm", "no_fm", "arow", "arow2", "mwbc", "a_go",
        "ssg", "a_ffn", "PS", "PSB", "onescol_b", "identf", "h128b", "hloc_d",
        "pp_d", "g_d", "xq_d", "x2_d", "qs_i"))
    wo_i = W_ORDER.index("wo")
    wait("sy", "sp_wr")

    for m in range(NMT):
        msl = slice(m * MT, (m + 1) * MT)
        if m > 0:
            wait("sy", "p3afin")
            wait("sy", "x2_wr", m * 16)
            wait("v", "xq_wr", m * 16)
        else:
            wait("sy", "carry_done")
        dma("sy", bigA[:].rearrange("p (j d) -> p j d", d=D),
            xo[msl, :].rearrange("(j p) d -> p j d", p=128), "x3_ld")
        dma("sy", yw[:].rearrange("p (s t) -> p s t", t=MT),
            hloc_d[:, :, msl].rearrange("s p t -> p s t"), "h3_ld")
        dma("sy", yw2[:].rearrange("p (s t) -> p s t", t=MT),
            pp_d[:, :, msl].rearrange("s p t -> p s t"), "h3_ld")
        o0 = b.val("o_done")
        g0 = b.val("g3_ld")
        for oc in range(16):
            if oc >= 2:
                wait("sy", "o_done", o0 + 3 * oc - 3)
            dma("sy", gst[:, (oc % 2) * MT:(oc % 2 + 1) * MT],
                g_d[oc, :, msl], "g3_ld")
        wait("v", "carry_done")
        wait("v", "h3_ld")
        op("v", "tensor_copy", st_b[0:1, 0:512], a_go[:, msl])
        op("v", "tensor_tensor", arow[0:1, :], st_b[0:1, 0:512],
           st_b[0:1, 0:512], A.mult)
        op("v", "tensor_tensor", arow[0:1, :], arow[0:1, :], ssg[:, msl], A.mult)
        op("v", "tensor_scalar", arow[0:1, :], arow[0:1, :], 1.0 / D, EPS_GN,
           A.mult, A.add).inc("gm_v", 1)
        wait("sc", "gm_v")
        op("sc", "activation", arow2[0:1, :], arow[0:1, :], AF.Sqrt,
           bias=0.0, scale=1.0).inc("gm_sq", 1)
        wait("v", "gm_sq")
        op("v", "reciprocal", arow[0:1, :], arow2[0:1, :])
        op("v", "tensor_tensor", arow2[0:1, :], arow[0:1, :],
           st_b[0:1, 0:512], A.mult)
        op("v", "tensor_scalar", arow[0:1, :], arow2[0:1, :], -1.0, None,
           A.mult).inc("gm_row", 1)
        _bcast_row(env, arow, brep, "bcg")
        s0 = b.val("o_si")
        t0 = b.val("osmm")
        for oc in range(16):
            sl = gst[:, (oc % 2) * MT:(oc % 2 + 1) * MT]
            hl = yw[:, oc * MT:(oc + 1) * MT]
            pp = yw2[:, oc * MT:(oc + 1) * MT]
            wait("v", "g3_ld", g0 + (oc + 1) * 16)
            if oc > 0:
                wait("v", "o_si", s0 + 2 * oc - 1)
            op("v", "scalar_tensor_tensor", st_a[:], pp, carry[:, oc:oc + 1],
               hl, A.mult, A.add).inc("o_fx", 1)
            wait("sc", "o_fx", o0 + 3 * oc + 1)
            op("sc", "activation", st_b[:], st_a[:], AF.Silu,
               bias=0.0, scale=1.0).inc("o_si", 1)
            wait("v", "o_si", s0 + 2 * oc + 1)
            if oc > 0:
                wait("v", "osq", s0 + 2 * oc)
            op("v", "scalar_tensor_tensor", st_c[:], sl, gnw_fm[:, oc:oc + 1],
               st_b[:], A.mult, A.mult)
            op("v", "tensor_tensor", st_d[:], st_c[:], brep[:],
               A.mult).inc("o_od", 1)
            wait("sc", "o_od", o0 + 3 * oc + 2)
            if oc > 0:
                wait("sc", "osmm", t0 + oc)
            op("sc", "activation", qtm[:, 0:512], st_d[:], AF.Square,
               bias=0.0, scale=1.0).inc("osq", 1)
            wait("te", "osq", s0 + 2 * oc + 2)
            op("te", "matmul", PS[6][0:1, 0:512], onescol_b[:], qtm[:, 0:512],
               start=(oc == 0), stop=(oc == 15)).inc("osmm", 1)
            op("v", "tensor_scalar", st_e[:], st_d[:], no_fm[:, oc:oc + 1],
               None, A.mult)
            op("v", "tensor_copy", hl, st_e[:])
            op("v", "tensor_scalar", st_c[:], st_e[:], -1.0, None, A.mult)
            op("v", "tensor_tensor", st_c[:], st_c[:], st_e[:], A.max)
            if oc == 0:
                op("v", "tensor_copy", st_f[:], st_c[:]).inc("o_done", 1)
            else:
                op("v", "tensor_tensor", st_f[:], st_f[:], st_c[:],
                   A.max).inc("o_done", 1)
        wait("v", "osmm", t0 + 16)
        op("v", "tensor_copy", arow2[0:1, :], PS[6][0:1, 0:512])
        op("v", "tensor_scalar", arow2[0:1, :], arow2[0:1, :], 1.0 / D, EPS_BL,
           A.mult, A.add).inc("or_v", 1)
        wait("sc", "or_v")
        op("sc", "activation", arow[0:1, :], arow2[0:1, :], AF.Sqrt,
           bias=0.0, scale=1.0).inc("or_sq", 1)
        wait("v", "or_sq")
        op("v", "reciprocal", arow2[0:1, :], arow[0:1, :])
        wait("te", "o_done", b.val("o_done"))

        hk_of = b.tinc("of_t")

        def th_ofm(te, hk=hk_of):
            for c in range(4):
                instr = te.transpose(PS[5][:, c * 128:(c + 1) * 128],
                                     st_f[:, c * 128:(c + 1) * 128], identf[:])
                if c == 3:
                    instr.then_inc(*hk)
        raw("te", th_ofm)
        wait("v", "of_t")
        op("v", "tensor_copy", st_a[:], PS[5][:, 0:512]).inc("of_tv", 1)
        env["b"].wait("te", "of_tv")
        for c in range(4):
            op("v", "tensor_reduce", col[:, c:c + 1],
               st_a[:, c * 128:(c + 1) * 128], AX, A.max)
        op("v", "nop").inc("omx_c", 1)
        _col_to_row(env, [col[:, c:c + 1] for c in range(4)], arow, "omx")
        op("v", "tensor_tensor", arow[0:1, :], arow[0:1, :], arow2[0:1, :], A.mult)
        op("v", "tensor_scalar", arow[0:1, :], arow[0:1, :], 1.0, CLIP_MIN,
           A.mult, A.max)
        op("v", "reciprocal", st_a[0:1, 0:512], arow[0:1, :])
        op("v", "tensor_tensor", st_a[0:1, 0:512], st_a[0:1, 0:512],
           arow2[0:1, :], A.mult)
        op("v", "tensor_scalar", st_a[0:1, 0:512], st_a[0:1, 0:512], 127.0,
           None, A.mult).inc("ob_row", 1)
        op("v", "tensor_scalar", arow[0:1, :], arow[0:1, :],
           mwbc[0:1, wo_i:wo_i + 1], -1.0 / 127.0, A.mult, A.mult).inc("oa_row", 1)
        _bcast_row(env, st_a[0:1, 0:512], brep2, "bco")
        _row_to_col(env, arow, [col[:, 20 + j:21 + j] for j in range(4)], "oac")
        for oc in range(16):
            u = yw[:, oc * MT:(oc + 1) * MT]
            op("v", "tensor_tensor", st_a[:], u, brep2[:], A.mult).inc("q3y", 1)
            wait("sc", "q3y")
            op("sc", "copy", qs_i[:, 0:512], st_a[:]).inc("q3c", 1)
            wait("v", "q3c")
            op("v", "tensor_tensor", st_b[:], st_a[:], qs_i[:, 0:512], A.subtract)
            op("v", "tensor_scalar", st_c[:], st_b[:], 0.5, None, A.is_gt)
            op("v", "tensor_tensor", st_d[:], qs_i[:, 0:512], st_c[:], A.add)
            op("v", "scalar_tensor_tensor", qT_a[:, oc * MT:(oc + 1) * MT],
               st_b[:], -0.5, st_d[:], A.is_lt, A.subtract).inc("q3d", 1)
        wbuf = env["wbuf"]
        wq_attn = env["wq_attn"]
        wl0 = b.val("wo_wl")
        mm0 = b.val("wo_mm")
        for idx in range(64):
            oc4, kc = divmod(idx, 16)
            if idx >= 8:
                wait("sy", "wo_mm", mm0 + (idx - 7) * 4)
            else:
                wait("sy", "wo_mm", b.val("wo_mm"))
            dma("sy", wbuf[:, (idx % 8) * 512:(idx % 8 + 1) * 512],
                wq_attn[wo_i, kc * 128:(kc + 1) * 128, oc4 * 512:(oc4 + 1) * 512],
                "wo_wl")
        wait("te", "q3d", b.val("q3d"))
        wait("v", "x3_ld")
        for idx in range(64):
            oc4, kc = divmod(idx, 16)
            wait("te", "wo_wl", wl0 + (idx + 1) * 16)
            hk = b.tinc("wo_mm", 4)

            def th(te, idx=idx, oc4=oc4, kc=kc, hk=hk):
                for j in range(4):
                    instr = te.matmul(PS[j][:, 0:512],
                                      qT_a[:, kc * MT + j * 128:kc * MT + (j + 1) * 128],
                                      wbuf[:, (idx % 8) * 512:(idx % 8 + 1) * 512],
                                      start=(kc == 0), stop=(kc == 15))
                    if j == 3:
                        instr.then_inc(*hk)
            raw("te", th)
            if kc == 15:
                for j in range(4):
                    wait("v", "wo_mm", b.val("wo_mm"))
                    blkA = bigA[:, j * D + oc4 * 512:j * D + (oc4 + 1) * 512]
                    op("v", "scalar_tensor_tensor", st_a[:], PS[j][:, 0:512],
                       col[:, 20 + j:21 + j],
                       rep["ga"][:, oc4 * 512:(oc4 + 1) * 512], A.mult, A.mult)
                    op("v", "tensor_tensor", blkA, st_a[:], blkA,
                       A.add).inc("wo_ev", 1)
                if oc4 < 3:
                    wait("te", "wo_ev", b.val("wo_ev"))
        wait("gp", "wo_ev", b.val("wo_ev"))
        dma("gp", x2_d[msl, :].rearrange("(j p) d -> p j d", p=128),
            bigA[:].rearrange("p (j d) -> p j d", d=D), "x2_wr")
        wait("sc", "wo_ev", b.val("wo_ev"))
        wait("v", "wo_ev", b.val("wo_ev"))
        for j in range(4):
            xj = bigA[:, j * D:(j + 1) * D]
            if j > 0:
                wait("sc", "q_done")
            op("sc", "activation", qs_t[:], xj, AF.Square, bias=0.0, scale=1.0,
               accum_out=col[:, 0:1]).inc("l2_sq", 1)
            op("v", "tensor_reduce", col[:, 1:2], xj, AX, A.add)
            wait("v", "l2_sq")
            op("v", "tensor_scalar", col[:, 2:3], col[:, 1:2], 1.0 / D, None, A.mult)
            op("v", "tensor_scalar", col[:, 3:4], col[:, 0:1], 1.0 / D, None, A.mult)
            op("v", "tensor_scalar", col[:, 4:5], col[:, 2:3], col[:, 2:3], None, A.mult)
            op("v", "tensor_tensor", col[:, 5:6], col[:, 3:4], col[:, 4:5], A.subtract)
            op("v", "tensor_scalar", col[:, 6:7], col[:, 5:6], EPS_LN, None,
               A.add).inc("l2_v", 1)
            wait("sc", "l2_v")
            op("sc", "activation", col[:, 7:8], col[:, 6:7], AF.Sqrt,
               bias=0.0, scale=1.0).inc("l2_sqrt", 1)
            wait("v", "l2_sqrt")
            op("v", "reciprocal", col[:, 8:9], col[:, 7:8])
            op("v", "tensor_scalar", col[:, 9:10], col[:, 2:3], col[:, 8:9],
               -1.0, A.mult, A.mult).inc("l2_ab", 1)
            wait("sc", "l2_ab")
            op("sc", "activation", qs_t[:], xj, AF.Identity, bias=col[:, 9:10],
               scale=col[:, 8:9]).inc("l2_t", 1)
            wait("v", "l2_t")
            op("v", "tensor_tensor", tsc[:], qs_t[:], rep["cb1"][:], A.mult)
            op("v", "tensor_tensor", qtm[:], tsc[:], rep["sb_"][:],
               A.add).inc("q_done", 1)
            _transpose_to_fm(env, qtm[:], yw, j)
            if DEBUG_DUMPS and m == 0:
                d_ywj = env["nc"].dram_tensor(f"dbg_ywj{j}", [128, 16 * MT],
                                              BF16, kind="ExternalOutput").ap()
                b.vbar("gp")
                dma("gp", d_ywj, yw[:], "ywdmp")
                wait("v", "ywdmp")
        wait("v", "tpe6")
        wait("v", "tpe7")
        if DEBUG_DUMPS and m == 0:
            d_yw = env["nc"].dram_tensor("dbg_yw", [128, 16 * MT], BF16,
                                         kind="ExternalOutput").ap()
            b.vbar("gp")
            dma("gp", d_yw, yw[:], "ywdmp")
            wait("v", "ywdmp")
        bufs = [yw, yw2]
        for stg in range(4):
            hh = 1 << stg
            srcb, dstb = bufs[stg % 2], bufs[(stg + 1) % 2]
            for a in range(16):
                if (a & hh) == 0:
                    sa = srcb[:, a * MT:(a + 1) * MT]
                    sb2 = srcb[:, (a + hh) * MT:(a + hh + 1) * MT]
                    op("v", "tensor_tensor", dstb[:, a * MT:(a + 1) * MT],
                       sa, sb2, A.add)
                    op("v", "tensor_tensor", dstb[:, (a + hh) * MT:(a + hh + 1) * MT],
                       sa, sb2, A.subtract)
        op("v", "nop").inc("bfly", 1)
        wait("te", "bfly")
        ffv0 = b.val("ff_ev")
        for oc in range(16):
            bk = oc % 2
            if oc >= 2:
                wait("te", "ff_ev", ffv0 + oc - 1)
            op("te", "matmul", PS[bk][:, 0:512], h128b[:],
               yw[:, oc * MT:(oc + 1) * MT], start=True, stop=True).inc("ff_mm", 1)
            wait("sc", "ff_mm")
            if oc > 0:
                wait("sc", "ff_ev", ffv0 + oc)
            op("sc", "activation", st_a[:], PS[bk][:, 0:512], AF.Abs,
               bias=0.0, scale=1.0).inc("ff_abs", 1)
            wait("v", "ff_abs")
            if oc == 0:
                op("v", "tensor_copy", st_f[:], st_a[:])
            else:
                op("v", "tensor_tensor", st_f[:], st_f[:], st_a[:], A.max)
            op("v", "tensor_copy", yw2[:, oc * MT:(oc + 1) * MT],
               PS[bk][:, 0:512]).inc("ff_ev", 1)
        op("v", "nop").inc("ff_fold", 1)
        wait("te", "ff_fold")

        hk_ff = b.tinc("ff_t")

        def th_ffm(te, hk=hk_ff):
            for c in range(4):
                instr = te.transpose(PS[5][:, c * 128:(c + 1) * 128],
                                     st_f[:, c * 128:(c + 1) * 128], identf[:])
                if c == 3:
                    instr.then_inc(*hk)
        raw("te", th_ffm)
        wait("v", "ff_t")
        op("v", "tensor_copy", st_a[:], PS[5][:, 0:512]).inc("ff_tv", 1)
        env["b"].wait("te", "ff_tv")
        for c in range(4):
            op("v", "tensor_reduce", col[:, c:c + 1],
               st_a[:, c * 128:(c + 1) * 128], AX, A.max)
        op("v", "nop").inc("fmx_c", 1)
        _col_to_row(env, [col[:, c:c + 1] for c in range(4)], arow, "fmx")
        op("v", "tensor_scalar", arow[0:1, :], arow[0:1, :], 1.0, CLIP_MIN,
           A.mult, A.max)
        op("v", "tensor_copy", a_ffn[:, msl], arow[0:1, :])
        op("v", "reciprocal", arow2[0:1, :], arow[0:1, :])
        op("v", "tensor_scalar", arow2[0:1, :], arow2[0:1, :], 127.0, None,
           A.mult).inc("fb_row", 1)
        _bcast_row(env, arow2, brep2, "bcff")
        for oc in range(16):
            u = yw2[:, oc * MT:(oc + 1) * MT]
            op("v", "tensor_tensor", st_a[:], u, brep2[:], A.mult).inc("q4y", 1)
            wait("sc", "q4y")
            op("sc", "copy", qs_i[:, 0:512], st_a[:]).inc("q4c", 1)
            wait("v", "q4c")
            op("v", "tensor_tensor", st_b[:], st_a[:], qs_i[:, 0:512], A.subtract)
            op("v", "tensor_scalar", st_c[:], st_b[:], 0.5, None, A.is_gt)
            op("v", "tensor_tensor", st_d[:], qs_i[:, 0:512], st_c[:], A.add)
            op("v", "scalar_tensor_tensor", qT_a[:, oc * MT:(oc + 1) * MT],
               st_b[:], -0.5, st_d[:], A.is_lt, A.subtract).inc("q4d", 1)
        wait("gp", "q4d", b.val("q4d"))
        dma("gp", xq_d[:, :, msl].rearrange("s p t -> p s t"),
            qT_a[:].rearrange("p (s t) -> p s t", t=MT), "xq_wr")
        op("v", "nop").inc("p3afin", 1)
    op("v", "nop").inc("p3a_all", 1)


def _phase3b(b, env):
    op, wait, dma, raw = b.op, b.wait, b.dma, b.raw
    g = env.__getitem__
    (col, qs_i, mwbc, a_ffn, PS, h128b, identf, wbuf, rep, xq_d, x2_d, y_out,
     wq_gate, wq_up, wq_down) = (g(k) for k in (
        "col", "qs_i", "mwbc", "a_ffn", "PS", "h128b", "identf", "wbuf",
        "rep", "xq_d", "x2_d", "y_out", "wq_gate", "wq_up", "wq_down"))
    nc = env["nc"]
    wait("v", "p3a_all")
    wait("sy", "p3a_all")
    wait("te", "p3a_all")
    wait("sc", "p3a_all")
    wait("gp", "p3a_all")
    wait("sy", "x2_wr")
    wait("sy", "xq_wr")
    env["es3a"].close()
    env["es1"].close()
    sbt = b.sb
    xqf = sbt("xqf", [128, 16 * MT], BF16)
    vfull = sbt("vfull", [128, 64 * MT], BF16)
    vstt = sbt("vstt", [128, MT], BF16)
    brepw = sbt("brepw", [128, MT], F32)
    brepw2 = sbt("brepw2", [128, MT], F32)
    aroww = sbt("aroww", [1, MT], F32)
    aroww2 = sbt("aroww2", [1, MT], F32)
    aroww3 = sbt("aroww3", [1, MT], F32)
    x2b = sbt("x2b", [128, 2 * 512], F32)
    wbufB = sbt("wbufB", [128, 8192], BF16)
    stw = [sbt(f"stw{i}", [128, MT], F32) for i in range(4)]
    env.update(dict(xqf=xqf, vfull=vfull, brepw=brepw, brepw2=brepw2,
                    aroww=aroww, aroww2=aroww2, x2b=x2b, stw=stw,
                    arow=aroww, arow2=aroww2))
    wg_i, wu_i, wd_i = (W_ORDER.index(w) for w in ("wgate", "wup", "wdown"))

    for f in range(4):
        fsl = slice(f * MT, (f + 1) * MT)
        if b.val("p3bfin") > 0:
            wait("sy", "p3bfin")
        dma("sy", xqf[:].rearrange("p (s t) -> p s t", t=MT),
            xq_d[:, :, fsl].rearrange("s p t -> p s t"), "xq_ld")
        wait("v", "xq_ld")
        op("v", "tensor_scalar", aroww[0:1, :], a_ffn[:, fsl],
           mwbc[0:1, wg_i:wg_i + 1], -ISQ_D / 127.0, A.mult,
           A.mult).inc("bg_row", 1)
        _bcast_row(env, aroww, brepw, "bcgg")
        e0 = b.val("fg_ev")
        m0 = b.val("fg_mm")
        for oc in range(64):
            for half, (pool, mtag) in enumerate(((wq_gate, "fg"), (wq_up, "fu"))):
                wsl = wbufB[:, half * 4096 + (oc % 2) * 2048:
                            half * 4096 + (oc % 2 + 1) * 2048]
                if oc >= 2:
                    wait("sy", mtag + "_mm", m0 + 2 * oc - 3 + half)
                else:
                    wait("sy", mtag + "_mm", b.val(mtag + "_mm"))
                dma("sy", wsl.rearrange("p (k c) -> p k c", c=128),
                    pool[oc].rearrange("(k p) c -> p k c", p=128), mtag + "_wl")
            for half, (pool, mtag) in enumerate(((wq_gate, "fg"), (wq_up, "fu"))):
                wsl = wbufB[:, half * 4096 + (oc % 2) * 2048:
                            half * 4096 + (oc % 2 + 1) * 2048]
                wait("te", mtag + "_wl", b.val(mtag + "_wl"))
                if oc >= 2:
                    wait("te", mtag + "_ev", e0 + 2 * oc - 3 + half)
                bk = (oc % 2) if half == 0 else 2 + (oc % 2)

                hk = b.tinc(mtag + "_mm")

                def th(te, wsl=wsl, bk=bk, hk=hk):
                    for kc in range(16):
                        instr = te.matmul(PS[bk][:, 0:512],
                                          wsl[:, kc * 128:(kc + 1) * 128],
                                          xqf[:, kc * MT:(kc + 1) * MT],
                                          start=(kc == 0), stop=(kc == 15))
                        if kc == 15:
                            instr.then_inc(*hk)
                raw("te", th)
            bkg, bku = oc % 2, 2 + (oc % 2)
            wait("v", "fg_mm")
            wait("v", "fu_mm")
            op("v", "tensor_tensor", stw[0][:], PS[bkg][:, 0:512],
               brepw[:], A.mult).inc("fg_ev", 1)
            wait("sc", "fg_ev")
            if oc > 0:
                wait("sc", "fu_ev", e0 + 2 * oc)
            op("sc", "activation", stw[1][:], stw[0][:],
               AF.Silu, bias=0.0, scale=1.0).inc("fe_si", 1)
            wait("v", "fe_si")
            op("v", "tensor_tensor", vfull[:, oc * MT:(oc + 1) * MT],
               stw[1][:], PS[bku][:, 0:512], A.mult).inc("fu_ev", 1)
        for stg in range(6):
            hh = 1 << stg
            for a in range(64):
                if (a & hh) == 0:
                    va = vfull[:, a * MT:(a + 1) * MT]
                    vb = vfull[:, (a + hh) * MT:(a + hh + 1) * MT]
                    op("v", "tensor_copy", vstt[:], va)
                    op("v", "tensor_tensor", va, vstt[:], vb, A.add)
                    op("v", "tensor_tensor", vb, vstt[:], vb, A.subtract)
        op("v", "nop").inc("bfly2", 1)
        wait("te", "bfly2")
        ev0 = b.val("fh_ev")
        for oc in range(64):
            bk = oc % 2
            if oc >= 2:
                wait("te", "fh_ev", ev0 + oc - 1)
            op("te", "matmul", PS[bk][:, 0:512], h128b[:],
               vfull[:, oc * MT:(oc + 1) * MT], start=True,
               stop=True).inc("fh_mm", 1)
            wait("sc", "fh_mm")
            if oc > 0:
                wait("sc", "fh_ev", ev0 + oc)
            op("sc", "activation", stw[0][:], PS[bk][:, 0:512],
               AF.Abs, bias=0.0, scale=1.0).inc("fh_abs", 1)
            wait("v", "fh_abs")
            if oc == 0:
                op("v", "tensor_copy", stw[3][:], stw[0][:])
            else:
                op("v", "tensor_tensor", stw[3][:], stw[3][:], stw[0][:], A.max)
            op("v", "tensor_copy", vfull[:, oc * MT:(oc + 1) * MT],
               PS[bk][:, 0:512]).inc("fh_ev", 1)
        op("v", "nop").inc("fh_fold", 1)
        wait("te", "fh_fold")

        hk_fh = b.tinc("fh_t")

        def th_fhm(te, hk=hk_fh):
            for c in range(4):
                instr = te.transpose(PS[5][:, c * 128:(c + 1) * 128],
                                     stw[3][:, c * 128:(c + 1) * 128], identf[:])
                if c == 3:
                    instr.then_inc(*hk)
        raw("te", th_fhm)
        wait("v", "fh_t")
        op("v", "tensor_copy", stw[0][:], PS[5][:, 0:512]).inc("fh_tv", 1)
        env["b"].wait("te", "fh_tv")
        for c in range(4):
            op("v", "tensor_reduce", col[:, c:c + 1],
               stw[0][:, c * 128:(c + 1) * 128], AX, A.max)
        op("v", "nop").inc("vmx_c", 1)
        _col_to_row(env, [col[:, c:c + 1] for c in range(4)], aroww, "vmx")
        op("v", "tensor_scalar", aroww[0:1, :], aroww[0:1, :], 1.0, CLIP_MIN,
           A.mult, A.max)
        op("v", "reciprocal", aroww2[0:1, :], aroww[0:1, :])
        op("v", "tensor_scalar", aroww2[0:1, :], aroww2[0:1, :], 127.0, None,
           A.mult).inc("vb_row", 1)
        _bcast_row(env, aroww2, brepw2, "bcvq")
        op("v", "tensor_scalar", aroww2[0:1, :], a_ffn[:, fsl],
           mwbc[0:1, wu_i:wu_i + 1], ISQ_D / 127.0, A.mult, A.mult)
        op("v", "tensor_tensor", aroww3[0:1, :], aroww[0:1, :], aroww2[0:1, :], A.mult)
        op("v", "tensor_scalar", aroww3[0:1, :], aroww3[0:1, :],
           mwbc[0:1, wd_i:wd_i + 1], ISQ_I / 127.0, A.mult,
           A.mult).inc("va_row", 1)
        _row_to_col(env, aroww3, [col[:, 12 + tt:13 + tt] for tt in range(4)], "vac")
        for oc in range(64):
            u = vfull[:, oc * MT:(oc + 1) * MT]
            op("v", "tensor_tensor", stw[0][:], u, brepw2[:], A.mult).inc("q5y", 1)
            wait("sc", "q5y")
            op("sc", "copy", qs_i[:, 0:512], stw[0][:]).inc("q5c", 1)
            wait("v", "q5c")
            op("v", "tensor_tensor", stw[1][:], stw[0][:], qs_i[:, 0:512],
               A.subtract)
            op("v", "tensor_scalar", stw[2][:], stw[1][:], 0.5, None, A.is_gt)
            op("v", "tensor_tensor", stw[3][:], qs_i[:, 0:512], stw[2][:], A.add)
            op("v", "scalar_tensor_tensor", u, stw[1][:], -0.5,
               stw[3][:], A.is_lt, A.subtract).inc("q5d", 1)
        wait("te", "q5d", b.val("q5d"))
        for oc4 in range(4):
            wl0 = b.val("dw_wl")
            mm0 = b.val("dw_mm")
            for kc in range(64):
                if kc >= 8:
                    wait("sy", "dw_mm", mm0 + kc - 7)
                else:
                    wait("sy", "dw_mm", b.val("dw_mm"))
                dma("sy", wbuf[:, (kc % 8) * 512:(kc % 8 + 1) * 512],
                    wq_down[oc4, kc * 128:(kc + 1) * 128, :], "dw_wl")
            for kc in range(64):
                wait("te", "dw_wl", wl0 + (kc + 1) * 16)

                hk = b.tinc("dw_mm")

                def th(te, kc=kc, hk=hk):
                    for tt in range(4):
                        instr = te.matmul(PS[tt][:, 0:512],
                                          vfull[:, kc * MT + tt * 128:kc * MT + (tt + 1) * 128],
                                          wbuf[:, (kc % 8) * 512:(kc % 8 + 1) * 512],
                                          start=(kc == 0), stop=(kc == 63))
                        if tt == 3:
                            instr.then_inc(*hk)
                raw("te", th)
            for tt in range(4):
                dma("sy", x2b[:, (tt % 2) * 512:(tt % 2 + 1) * 512],
                    x2_d[f * MT + tt * 128:f * MT + (tt + 1) * 128,
                         oc4 * 512:(oc4 + 1) * 512], "x2_ld")
                wait("v", "x2_ld")
                wait("v", "dw_mm", b.val("dw_mm"))
                acol = col[:, 12 + tt:13 + tt]
                op("v", "scalar_tensor_tensor", stw[0][:], PS[tt][:, 0:512],
                   acol, rep["gb"][:, oc4 * 512:(oc4 + 1) * 512],
                   A.mult, A.mult)
                op("v", "tensor_tensor", stw[1][:], stw[0][:],
                   x2b[:, (tt % 2) * 512:(tt % 2 + 1) * 512],
                   A.add).inc("dw_ev", 1)
                wait("sy", "dw_ev")
                dma("sy", y_out[f * MT + tt * 128:f * MT + (tt + 1) * 128,
                                oc4 * 512:(oc4 + 1) * 512],
                    stw[1][:], "y_wr")
                wait("v", "y_wr")
            wait("te", "dw_ev", b.val("dw_ev"))
        op("v", "nop").inc("p3bfin", 1)
    op("v", "nop").inc("p3b_all", 1)


def _finish(b, env):
    op, wait, dma = b.op, b.wait, b.dma
    y_out, bigA, xo = env["y_out"], env["bigA"], env["xo"]
    dbg = env["dbg"]
    if STAGE < 4:
        wait("sy", "carry_done" if STAGE >= 2 else "reps_done")
        for m2 in range(NMT):
            dma("sy", bigA[:].rearrange("p (j d) -> p j d", d=D),
                xo[m2 * MT:(m2 + 1) * MT, :].rearrange("(j p) d -> p j d", p=128),
                "yo_ld")
            wait("sy", "yo_ld")
            dma("sy", y_out[m2 * MT:(m2 + 1) * MT, :]
                .rearrange("(j p) d -> p j d", p=128),
                bigA[:].rearrange("p (j d) -> p j d", d=D), "yo_st")
            wait("sy", "yo_st")
    if DEBUG_DUMPS:
        nc = env["nc"]
        EO = lambda n, s, t: nc.dram_tensor(n, s, t, kind="ExternalOutput").ap()
        wait("gp", "vcm")
        if STAGE >= 2:
            wait("gp", "sp_wr")
            d_mw = EO("dbg_mwbc", [128, 8], F32)
            dma("gp", d_mw, env["mwbc"][:], "dbg_st")
            d_wr = EO("dbg_wrow", [1, 48], F32)
            dma("gp", d_wr, env["wrow"][0:1, :], "dbg_st")
            d_wq = EO("dbg_wq", [4, D, D], BF16)
            dma("gp", d_wq, env["wq_attn"][:, :, :], "dbg_st")
            d_hl = EO("dbg_hl", [16, 128, TOK], BF16)
            d_pp = EO("dbg_pp", [16, 128, TOK], BF16)
            d_g = EO("dbg_g", [16, 128, TOK], BF16)
            d_sg = EO("dbg_sg", [2, D], F32)
            d_carry = EO("dbg_carry", [128, 16], F32)
            dma("gp", d_hl, env["hloc_d"], "dbg_st")
            dma("gp", d_pp, env["pp_d"], "dbg_st")
            dma("gp", d_g, env["g_d"], "dbg_st")
            if STAGE < 4:
                dma("gp", d_sg[0:1, :], env["ssg"], "dbg_st")
                dma("gp", d_sg[1:2, :], env["a_go"], "dbg_st")
                dma("gp", d_carry, env["carry"][:], "dbg_st")
        if STAGE >= 3:
            wait("gp", "x2_wr")
            wait("gp", "xq_wr")
            d_x2 = EO("dbg_x2", [TOK, D], F32)
            d_xq = EO("dbg_xq", [16, 128, TOK], BF16)
            d_af = EO("dbg_af", [1, D], F32)
            dma("gp", d_x2, env["x2_d"], "dbg_st")
            dma("gp", d_xq, env["xq_d"], "dbg_st")
            dma("gp", d_af, env["a_ffn"][0:1, :], "dbg_st")
            d_vecs = EO("dbg_vecs", [6, D], F32)
            dma("gp", d_vecs, env["vecs_d"], "dbg_st")
            if STAGE < 4:
                d_rcb = EO("dbg_rcb", [128, D], BF16)
                d_rsb = EO("dbg_rsb", [128, D], BF16)
                dma("gp", d_rcb, env["rep"]["cb1"][:], "dbg_st")
                dma("gp", d_rsb, env["rep"]["sb_"][:], "dbg_st")
        wait("gp", "dbg_st")
    for name, (h, cnt) in list(b.sems.items()):
        if cnt > 0:
            Raw(b, "gp", lambda e, h=h, c=cnt: e.wait_ge(h, c))
    Raw(b, "gp", lambda e: e.drain())
    if STAGE < 4:
        env["es3a"].close()
        env["es1"].close()


def _host_inputs(inputs):
    x = np.ascontiguousarray(np.asarray(inputs["L0_attn_hidden_states"], np.float32))
    WT = {
        "wi": np.ascontiguousarray(np.asarray(inputs["L0_attn_Wi"], np.float32).T),
        "wf": np.ascontiguousarray(np.asarray(inputs["L0_attn_Wf"], np.float32).T),
        "wg": np.ascontiguousarray(np.asarray(inputs["L0_attn_Wg"], np.float32).T),
        "wo": np.ascontiguousarray(np.asarray(inputs["L0_attn_Wo"], np.float32).T),
        "wgate": np.ascontiguousarray(np.asarray(inputs["L0_ffn_w_gate"], np.float32).T),
        "wup": np.ascontiguousarray(np.asarray(inputs["L0_ffn_w_up"], np.float32).T),
        "wdown": np.ascontiguousarray(np.asarray(inputs["L0_ffn_w_down"], np.float32).T),
    }
    WTF = {k: v.reshape(-1, D) for k, v in WT.items()}
    norms = np.stack([np.asarray(inputs["L0_attn_norm_i"], np.float32),
                      np.asarray(inputs["L0_attn_norm_f"], np.float32),
                      np.asarray(inputs["L0_attn_norm_g"], np.float32),
                      np.asarray(inputs["L0_attn_norm_o"], np.float32),
                      np.asarray(inputs["L0_attn_g_norm_w"], np.float32)], axis=0)
    wmod = np.ascontiguousarray(np.asarray(inputs["L0_wmod"], np.float32))
    cv = np.ascontiguousarray(np.asarray(inputs["L0_c"], np.float32))
    constf = np.zeros((128, 256), np.float32)
    constf[:, 0:128] = np.eye(128, dtype=np.float32)
    constf[:, 128:256] = _hadamard(128)

    in_maps = []
    for k in range(8):
        role = k % 2
        seq = k // 4
        halfpair = (k // 2) % 2
        chunk = 2 * halfpair + role
        own = x[seq, chunk * TOK:(chunk + 1) * TOK, :]
        auxr = x[seq, role * TOK:(role + 1) * TOK, :]
        qparts = []
        for w in W_ORDER:
            if w in ("wgate", "wup"):
                arr = WT[w]
                rows = [arr[role * 1024:(role + 1) * 1024,
                            s * 2048:(s + 1) * 2048] for s in range(4)]
                qparts.append(np.concatenate(rows, axis=0))
            else:
                n = WTF[w].shape[0]
                h = n // 2
                qparts.append(WTF[w][role * h:role * h + QROWS[w]])
        qsrc = np.ascontiguousarray(np.concatenate(qparts, axis=0))
        maskm = np.zeros((128, 8), np.float32)
        late = halfpair == 1
        maskm[:, 0] = 1.0 if late else 0.0
        maskm[:, 1] = 1.0 if role == 0 else 0.0
        maskm[:, 2] = 1.0 if role == 1 else 0.0
        maskm[:, 3] = 1.0 if seq == 0 else 0.0
        maskm[:, 4] = 1.0 if seq == 1 else 0.0
        maskm[:, 5] = 1.0
        in_maps.append({
            "xo": np.ascontiguousarray(own),
            "xa": np.ascontiguousarray(auxr),
            "qsrc": qsrc,
            "wmod_sl": np.ascontiguousarray(wmod[:, role * 6144:(role + 1) * 6144]),
            "cvec": cv,
            "norms5": norms,
            "constf": constf,
            "maskv": maskm,
            "rolei": np.array([[role] + [0] * 7], np.int32),
        })
    return in_maps


def _assemble(results):
    out = np.zeros((2, L_ := 8192, D), np.float32)
    for k in range(8):
        role = k % 2
        seq = k // 4
        halfpair = (k // 2) % 2
        chunk = 2 * halfpair + role
        out[seq, chunk * TOK:(chunk + 1) * TOK, :] = np.asarray(results[k]["y"])
    return out


_CACHE = {}


def kernel(**inputs):
    if "nc" not in _CACHE:
        _CACHE["nc"], _ = build_program()
    nc = _CACHE["nc"]
    in_maps = _host_inputs(inputs)
    res = run_bass_kernel_spmd(nc, in_maps, core_ids=list(range(8)))
    return _assemble(res.results)
